# version 01 (final)
# baseline (speedup 1.0000x reference)
import numpy as np
import contextlib
import concourse.bass as bass
import concourse.mybir as mybir
from concourse.bass_utils import run_bass_kernel_spmd

F32 = mybir.dt.float32
BF16 = mybir.dt.bfloat16
AF = mybir.ActivationFunctionType
ALU = mybir.AluOpType
AX = mybir.AxisListType

NCORES = 8
T = 16384
D = 2048
TC = T // NCORES
MAGIC = 8388608.0
TWO_PI = 6.283185307179586


class Ser:

    def __init__(self, nc):
        self.nc = nc
        self.ops = []

    def op(self, eng, fn, inc=1):
        self.ops.append((eng, fn, inc))

    def dma(self, out, in_):
        self.op('sync', lambda p, out=out, in_=in_: p.dma_start(out=out, in_=in_), 16)

    def mm(self, out, lhsT, rhs, start=True, stop=True):
        self.op('tensor', lambda p, o=out, l=lhsT, r=rhs, s=start, st=stop:
                p.matmul(o, l, r, start=s, stop=st))

    def act(self, out, in_, func, bias=0.0, scale=1.0):
        self.op('scalar', lambda p, o=out, i=in_, f=func, b=bias, s=scale:
                p.activation(o, i, f, bias=b, scale=s))

    def tt(self, out, a, b, alu):
        self.op('vector', lambda p, o=out, x=a, y=b, al=alu: p.tensor_tensor(o, x, y, al))

    def ts(self, out, a, scalar, alu):
        self.op('vector', lambda p, o=out, x=a, s=scalar, al=alu:
                p.tensor_single_scalar(o, x, s, al))

    def stt(self, out, in0, scalar, in1, op0, op1):
        self.op('vector', lambda p, o=out, i0=in0, s=scalar, i1=in1, a=op0, b=op1:
                p.scalar_tensor_tensor(o, i0, s, i1, a, b))

    def copy(self, out, in_):
        self.op('vector', lambda p, o=out, i=in_: p.tensor_copy(o, i))

    def red(self, out, in_):
        self.op('vector', lambda p, o=out, i=in_: p.reduce_sum(o, i, axis=AX.X))

    def memset(self, ap, v):
        self.op('vector', lambda p, a=ap, vv=v: p.memset(a, vv))

    def build(self):
        nc = self.nc
        starts = []
        tot = 0
        for _, _, inc in self.ops:
            starts.append(tot)
            tot += inc
        engines = []
        for e in ('sync', 'tensor', 'vector', 'scalar', 'gpsimd'):
            if any(o[0] == e for o in self.ops):
                engines.append(e)
        with nc.semaphore() as tok, nc.Block() as block:
            for ename in engines:
                def mkbody(ename):
                    def body(proxy):
                        for (e, fn, inc), st in zip(self.ops, starts):
                            if e != ename:
                                continue
                            if st > 0:
                                proxy.wait_ge(tok, st)
                            fn(proxy).then_inc(tok, inc)
                        if ename == 'sync':
                            proxy.wait_ge(tok, tot)
                    return body
                getattr(block, ename)(mkbody(ename))
        return nc


_prog_cache = {}


def _launch(key, builder, in_maps):
    if key not in _prog_cache:
        _prog_cache[key] = builder()
    nc = _prog_cache[key]
    res = run_bass_kernel_spmd(nc, in_maps, core_ids=list(range(NCORES)))
    return res.results


def cc(a):
    return np.ascontiguousarray(np.asarray(a, dtype=np.float32))


def build_mm(K, N, Mc, func):
    nc = bass.Bass("TRN2", target_bir_lowering=False)
    xt = nc.dram_tensor("xt", [K, Mc], F32, kind="ExternalInput").ap()
    w = nc.dram_tensor("w", [K, N], F32, kind="ExternalInput").ap()
    bvec = nc.dram_tensor("bvec", [N, 1], F32, kind="ExternalInput").ap()
    yt = nc.dram_tensor("yt", [N, Mc], F32, kind="ExternalOutput").ap()
    KT = K // 128
    NT = (N + 127) // 128
    mw = min(512, Mc)
    MT = Mc // mw
    s = Ser(nc)
    with contextlib.ExitStack() as st:
        xstage = st.enter_context(nc.sbuf_tensor("xstage", [128, Mc], F32))
        xb = [st.enter_context(nc.sbuf_tensor(f"xb{k}", [128, Mc], BF16)) for k in range(KT)]
        wst = st.enter_context(nc.sbuf_tensor("wst", [128, 128], F32))
        wb = st.enter_context(nc.sbuf_tensor("wb", [128, 128], BF16))
        osb = st.enter_context(nc.sbuf_tensor("osb", [128, mw], F32))
        bcol = st.enter_context(nc.sbuf_tensor("bcol", [128, 1], F32))
        ps = [st.enter_context(nc.psum_tensor(f"ps{i}", [128, mw], F32)) for i in range(MT)]
        for k in range(KT):
            s.dma(xstage[:, :], xt[k * 128:(k + 1) * 128, :])
            s.copy(xb[k][:, :], xstage[:, :])
        for nt in range(NT):
            nr = min(128, N - nt * 128)
            s.dma(bcol[:nr, :], bvec[nt * 128:nt * 128 + nr, :])
            for k in range(KT):
                s.dma(wst[:, :nr], w[k * 128:(k + 1) * 128, nt * 128:nt * 128 + nr])
                s.copy(wb[:, :nr], wst[:, :nr])
                for mt in range(MT):
                    s.mm(ps[mt][:nr, :], wb[:, :nr], xb[k][:, mt * mw:(mt + 1) * mw],
                         start=(k == 0), stop=(k == KT - 1))
            for mt in range(MT):
                if func == AF.Copy:
                    s.ts(osb[:nr, :], ps[mt][:nr, :], bcol[:nr, :], ALU.add)
                else:
                    s.act(osb[:nr, :], ps[mt][:nr, :], func, bias=bcol[:nr, :])
                s.dma(yt[nt * 128:nt * 128 + nr, mt * mw:(mt + 1) * mw], osb[:nr, :])
        s.build()
    return nc


def mm_shardM(x, w, b, func=AF.Copy):
    M, K = x.shape
    N = w.shape[1]
    Mc = M // NCORES
    xt = cc(x.T)
    wc = cc(w)
    bc = cc(np.reshape(b, (N, 1)))
    ins = [{"xt": cc(xt[:, i * Mc:(i + 1) * Mc]), "w": wc, "bvec": bc}
           for i in range(NCORES)]
    res = _launch(("mm", K, N, Mc, int(func)), lambda: build_mm(K, N, Mc, func), ins)
    return np.concatenate([r["yt"] for r in res], axis=1).T


def mm_shardN(xrow, w, b, func=AF.Copy):
    K = xrow.shape[0]
    N = w.shape[1]
    Nc = N // NCORES
    xt = cc(np.reshape(xrow, (K, 1)))
    ins = [{"xt": xt, "w": cc(w[:, i * Nc:(i + 1) * Nc]),
            "bvec": cc(np.reshape(b[i * Nc:(i + 1) * Nc], (Nc, 1)))}
           for i in range(NCORES)]
    res = _launch(("mm", K, Nc, 1, int(func)), lambda: build_mm(K, Nc, 1, func), ins)
    return np.concatenate([r["yt"][:, 0] for r in res], axis=0)


def build_ln(R, Dm, use_gb, eps):
    nc = bass.Bass("TRN2", target_bir_lowering=False)
    x = nc.dram_tensor("x", [R, Dm], F32, kind="ExternalInput").ap()
    y = nc.dram_tensor("y", [R, Dm], F32, kind="ExternalOutput").ap()
    if use_gb:
        g = nc.dram_tensor("g", [128, Dm], F32, kind="ExternalInput").ap()
        be = nc.dram_tensor("be", [128, Dm], F32, kind="ExternalInput").ap()
    s = Ser(nc)
    with contextlib.ExitStack() as st:
        xt = st.enter_context(nc.sbuf_tensor("xt", [128, Dm], F32))
        ct = st.enter_context(nc.sbuf_tensor("ct", [128, Dm], F32))
        sq = st.enter_context(nc.sbuf_tensor("sq", [128, Dm], F32))
        mn = st.enter_context(nc.sbuf_tensor("mn", [128, 1], F32))
        vr = st.enter_context(nc.sbuf_tensor("vr", [128, 1], F32))
        if use_gb:
            gs = st.enter_context(nc.sbuf_tensor("gs", [128, Dm], F32))
            bs = st.enter_context(nc.sbuf_tensor("bs", [128, Dm], F32))
            s.dma(gs[:, :], g[:, :])
            s.dma(bs[:, :], be[:, :])
        nt = (R + 127) // 128
        for i in range(nt):
            r = min(128, R - i * 128)
            s.dma(xt[:r, :], x[i * 128:i * 128 + r, :])
            s.red(mn[:r, :], xt[:r, :])
            s.op('scalar', lambda p, o=mn[:r, :]: p.mul(o, o, 1.0 / Dm))
            s.ts(ct[:r, :], xt[:r, :], mn[:r, :], ALU.subtract)
            s.act(sq[:r, :], ct[:r, :], AF.Square)
            s.red(vr[:r, :], sq[:r, :])
            s.op('scalar', lambda p, o=vr[:r, :]: p.mul(o, o, 1.0 / Dm))
            s.ts(vr[:r, :], vr[:r, :], eps, ALU.add)
            s.act(vr[:r, :], vr[:r, :], AF.Ln)
            s.act(vr[:r, :], vr[:r, :], AF.Exp, scale=-0.5)
            s.ts(ct[:r, :], ct[:r, :], vr[:r, :], ALU.mult)
            if use_gb:
                s.tt(ct[:r, :], ct[:r, :], gs[:r, :], ALU.mult)
                s.tt(ct[:r, :], ct[:r, :], bs[:r, :], ALU.add)
            s.dma(y[i * 128:i * 128 + r, :], ct[:r, :])
        s.build()
    return nc


def ln_shard(x, gamma=None, beta=None, eps=1e-5):
    M, Dm = x.shape
    Mc = M // NCORES
    use_gb = gamma is not None
    ins = []
    for i in range(NCORES):
        d = {"x": cc(x[i * Mc:(i + 1) * Mc])}
        if use_gb:
            d["g"] = cc(np.tile(np.reshape(gamma, (1, Dm)), (128, 1)))
            d["be"] = cc(np.tile(np.reshape(beta, (1, Dm)), (128, 1)))
        ins.append(d)
    res = _launch(("ln", Mc, Dm, use_gb, eps), lambda: build_ln(Mc, Dm, use_gb, eps), ins)
    return np.concatenate([r["y"] for r in res], axis=0)


def ln_small(x, gamma=None, beta=None, eps=1e-5):
    R, Dm = x.shape
    use_gb = gamma is not None
    d = {"x": cc(x)}
    if use_gb:
        d["g"] = cc(np.tile(np.reshape(gamma, (1, Dm)), (128, 1)))
        d["be"] = cc(np.tile(np.reshape(beta, (1, Dm)), (128, 1)))
    res = _launch(("ln", R, Dm, use_gb, eps), lambda: build_ln(R, Dm, use_gb, eps),
                  [d] * NCORES)
    return res[0]["y"]


def build_axpy(R, C, alpha):
    nc = bass.Bass("TRN2", target_bir_lowering=False)
    a = nc.dram_tensor("a", [R, C], F32, kind="ExternalInput").ap()
    b = nc.dram_tensor("b", [R, C], F32, kind="ExternalInput").ap()
    y = nc.dram_tensor("y", [R, C], F32, kind="ExternalOutput").ap()
    s = Ser(nc)
    with contextlib.ExitStack() as st:
        at = st.enter_context(nc.sbuf_tensor("at", [128, C], F32))
        bt = st.enter_context(nc.sbuf_tensor("bt", [128, C], F32))
        nt = (R + 127) // 128
        for i in range(nt):
            r = min(128, R - i * 128)
            s.dma(at[:r, :], a[i * 128:i * 128 + r, :])
            s.dma(bt[:r, :], b[i * 128:i * 128 + r, :])
            s.stt(at[:r, :], bt[:r, :], alpha, at[:r, :], ALU.mult, ALU.add)
            s.dma(y[i * 128:i * 128 + r, :], at[:r, :])
        s.build()
    return nc


def axpy(a, b, alpha):
    R, C = a.shape
    if R % NCORES == 0 and R > 128:
        Rc = R // NCORES
        ins = [{"a": cc(a[i * Rc:(i + 1) * Rc]), "b": cc(b[i * Rc:(i + 1) * Rc])}
               for i in range(NCORES)]
        res = _launch(("axpy", Rc, C, alpha), lambda: build_axpy(Rc, C, alpha), ins)
        return np.concatenate([r["y"] for r in res], axis=0)
    ins = [{"a": cc(a), "b": cc(b)}] * NCORES
    res = _launch(("axpy", R, C, alpha), lambda: build_axpy(R, C, alpha), ins)
    return res[0]["y"]


def build_ffncomb(R, Dm, eps):
    nc = bass.Bass("TRN2", target_bir_lowering=False)
    xf = nc.dram_tensor("xf", [R, Dm], F32, kind="ExternalInput").ap()
    gt = nc.dram_tensor("gt", [R, Dm], F32, kind="ExternalInput").ap()
    bct = nc.dram_tensor("bct", [128, Dm], F32, kind="ExternalInput").ap()
    og = nc.dram_tensor("og", [128, Dm], F32, kind="ExternalInput").ap()
    ob = nc.dram_tensor("ob", [128, Dm], F32, kind="ExternalInput").ap()
    y = nc.dram_tensor("y", [R, Dm], F32, kind="ExternalOutput").ap()
    s = Ser(nc)
    with contextlib.ExitStack() as st:
        xt = st.enter_context(nc.sbuf_tensor("xt", [128, Dm], F32))
        gg = st.enter_context(nc.sbuf_tensor("gg", [128, Dm], F32))
        bcs = st.enter_context(nc.sbuf_tensor("bcs", [128, Dm], F32))
        ogs = st.enter_context(nc.sbuf_tensor("ogs", [128, Dm], F32))
        obs = st.enter_context(nc.sbuf_tensor("obs", [128, Dm], F32))
        sq = st.enter_context(nc.sbuf_tensor("sq", [128, Dm], F32))
        mn = st.enter_context(nc.sbuf_tensor("mn", [128, 1], F32))
        vr = st.enter_context(nc.sbuf_tensor("vr", [128, 1], F32))
        s.dma(bcs[:, :], bct[:, :])
        s.dma(ogs[:, :], og[:, :])
        s.dma(obs[:, :], ob[:, :])
        nt = (R + 127) // 128
        for i in range(nt):
            r = min(128, R - i * 128)
            s.dma(xt[:r, :], xf[i * 128:i * 128 + r, :])
            s.dma(gg[:r, :], gt[i * 128:i * 128 + r, :])
            s.tt(gg[:r, :], gg[:r, :], bcs[:r, :], ALU.mult)
            s.tt(xt[:r, :], xt[:r, :], gg[:r, :], ALU.add)
            s.red(mn[:r, :], xt[:r, :])
            s.op('scalar', lambda p, o=mn[:r, :]: p.mul(o, o, 1.0 / Dm))
            s.ts(xt[:r, :], xt[:r, :], mn[:r, :], ALU.subtract)
            s.act(sq[:r, :], xt[:r, :], AF.Square)
            s.red(vr[:r, :], sq[:r, :])
            s.op('scalar', lambda p, o=vr[:r, :]: p.mul(o, o, 1.0 / Dm))
            s.ts(vr[:r, :], vr[:r, :], eps, ALU.add)
            s.act(vr[:r, :], vr[:r, :], AF.Ln)
            s.act(vr[:r, :], vr[:r, :], AF.Exp, scale=-0.5)
            s.ts(xt[:r, :], xt[:r, :], vr[:r, :], ALU.mult)
            s.tt(xt[:r, :], xt[:r, :], ogs[:r, :], ALU.mult)
            s.tt(xt[:r, :], xt[:r, :], obs[:r, :], ALU.add)
            s.dma(y[i * 128:i * 128 + r, :], xt[:r, :])
        s.build()
    return nc


def ffn_block(xin, W):
    xlast = xin[T - 1:T, :]
    xn_last = ln_small(xlast, W["in_gamma"], W["in_beta"], eps=1e-6)
    thought = mm_shardN(xn_last[0], W["agg_w"], W["agg_b"])
    for _ in range(4):
        tln = ln_small(np.reshape(thought, (1, D)), W["pn_gamma"], W["pn_beta"], eps=1e-6)
        h1 = mm_shardN(tln[0], W["d1_w"], W["d1_b"], func=AF.Gelu)
        h2 = mm_shardN(h1, W["d2_w"], W["d2_b"])
        thought = axpy(np.reshape(thought, (1, D)), np.reshape(h2, (1, D)), 1.0)[0]
    bc = mm_shardN(thought, W["bc_w"], W["bc_b"])
    gate = mm_shardM(xin, W["gate_w"], W["gate_b"], func=AF.Sigmoid)
    Mc = T // NCORES
    bct = cc(np.tile(np.reshape(bc, (1, D)), (128, 1)))
    ogt = cc(np.tile(np.reshape(W["out_gamma"], (1, D)), (128, 1)))
    obt = cc(np.tile(np.reshape(W["out_beta"], (1, D)), (128, 1)))
    ins = [{"xf": cc(xin[i * Mc:(i + 1) * Mc]), "gt": cc(gate[i * Mc:(i + 1) * Mc]),
            "bct": bct, "og": ogt, "ob": obt} for i in range(NCORES)]
    res = _launch(("ffncomb", Mc, D), lambda: build_ffncomb(Mc, D, 1e-6), ins)
    return np.concatenate([r["y"] for r in res], axis=0)


def build_softplus(R):
    nc = bass.Bass("TRN2", target_bir_lowering=False)
    x = nc.dram_tensor("x", [R, 64], F32, kind="ExternalInput").ap()
    bt = nc.dram_tensor("bt", [128, 64], F32, kind="ExternalInput").ap()
    y = nc.dram_tensor("y", [R, 64], F32, kind="ExternalOutput").ap()
    s = Ser(nc)
    with contextlib.ExitStack() as st:
        xt = st.enter_context(nc.sbuf_tensor("xt", [128, 64], F32))
        bs = st.enter_context(nc.sbuf_tensor("bs", [128, 64], F32))
        tn = st.enter_context(nc.sbuf_tensor("tn", [128, 64], F32))
        ab = st.enter_context(nc.sbuf_tensor("ab", [128, 64], F32))
        rl = st.enter_context(nc.sbuf_tensor("rl", [128, 64], F32))
        s.dma(bs[:, :], bt[:, :])
        for i in range(R // 128):
            s.dma(xt[:, :], x[i * 128:(i + 1) * 128, :])
            s.tt(xt[:, :], xt[:, :], bs[:, :], ALU.add)
            s.act(tn[:, :], xt[:, :], AF.Copy, scale=-1.0)
            s.tt(ab[:, :], xt[:, :], tn[:, :], ALU.max)
            s.act(ab[:, :], ab[:, :], AF.Exp, scale=-1.0)
            s.act(ab[:, :], ab[:, :], AF.Ln, bias=1.0)
            s.ts(rl[:, :], xt[:, :], 0.0, ALU.max)
            s.tt(rl[:, :], rl[:, :], ab[:, :], ALU.add)
            s.dma(y[i * 128:(i + 1) * 128, :], rl[:, :])
        s.build()
    return nc


def build_cumloc(R):
    nc = bass.Bass("TRN2", target_bir_lowering=False)
    x = nc.dram_tensor("x", [R, 64], F32, kind="ExternalInput").ap()
    tri = nc.dram_tensor("tri", [128, 128], F32, kind="ExternalInput").ap()
    onc = nc.dram_tensor("onc", [128, 1], F32, kind="ExternalInput").ap()
    onr = nc.dram_tensor("onr", [1, 128], F32, kind="ExternalInput").ap()
    y = nc.dram_tensor("y", [R, 64], F32, kind="ExternalOutput").ap()
    tout = nc.dram_tensor("tout", [1, 64], F32, kind="ExternalOutput").ap()
    s = Ser(nc)
    with contextlib.ExitStack() as st:
        xt = st.enter_context(nc.sbuf_tensor("xt", [128, 64], F32))
        tr = st.enter_context(nc.sbuf_tensor("tr", [128, 128], F32))
        oc = st.enter_context(nc.sbuf_tensor("oc", [128, 1], F32))
        orr = st.enter_context(nc.sbuf_tensor("orr", [1, 128], F32))
        off = st.enter_context(nc.sbuf_tensor("off", [1, 64], F32))
        cs = st.enter_context(nc.sbuf_tensor("cs", [128, 64], F32))
        ob = st.enter_context(nc.sbuf_tensor("ob", [128, 64], F32))
        pc = st.enter_context(nc.psum_tensor([128, 64], F32))
        pt = st.enter_context(nc.psum_tensor([1, 64], F32))
        pb = st.enter_context(nc.psum_tensor([128, 64], F32))
        s.dma(tr[:, :], tri[:, :])
        s.dma(oc[:, :], onc[:, :])
        s.dma(orr[:, :], onr[:, :])
        s.memset(off[:, :], 0.0)
        for i in range(R // 128):
            s.dma(xt[:, :], x[i * 128:(i + 1) * 128, :])
            s.mm(pc[:, :], tr[:, :], xt[:, :])
            s.mm(pb[:, :], orr[:, :], off[:, :])
            s.copy(cs[:, :], pc[:, :])
            s.tt(ob[:, :], cs[:, :], pb[:, :], ALU.add)
            s.dma(y[i * 128:(i + 1) * 128, :], ob[:, :])
            s.mm(pt[:, :], oc[:, :], xt[:, :])
            s.tt(off[:, :], off[:, :], pt[:, :], ALU.add)
        s.dma(tout[:, :], off[:, :])
        s.build()
    return nc


def build_cumglob(R):
    nc = bass.Bass("TRN2", target_bir_lowering=False)
    x = nc.dram_tensor("x", [R, 64], F32, kind="ExternalInput").ap()
    tots = nc.dram_tensor("tots", [8, 64], F32, kind="ExternalInput").ap()
    msk = nc.dram_tensor("msk", [8, 1], F32, kind="ExternalInput").ap()
    onr = nc.dram_tensor("onr", [1, 128], F32, kind="ExternalInput").ap()
    y = nc.dram_tensor("y", [R, 64], F32, kind="ExternalOutput").ap()
    s = Ser(nc)
    with contextlib.ExitStack() as st:
        tt_ = st.enter_context(nc.sbuf_tensor("tt", [8, 64], F32))
        ms = st.enter_context(nc.sbuf_tensor("ms", [8, 1], F32))
        orr = st.enter_context(nc.sbuf_tensor("orr", [1, 128], F32))
        offs = st.enter_context(nc.sbuf_tensor("offs", [1, 64], F32))
        xt = st.enter_context(nc.sbuf_tensor("xt", [128, 64], F32))
        po = st.enter_context(nc.psum_tensor([1, 64], F32))
        pb = st.enter_context(nc.psum_tensor([128, 64], F32))
        s.dma(tt_[:, :], tots[:, :])
        s.dma(ms[:, :], msk[:, :])
        s.dma(orr[:, :], onr[:, :])
        s.mm(po[:, :], ms[:, :], tt_[:, :])
        s.copy(offs[:, :], po[:, :])
        s.mm(pb[:, :], orr[:, :], offs[:, :])
        for i in range(R // 128):
            s.dma(xt[:, :], x[i * 128:(i + 1) * 128, :])
            s.tt(xt[:, :], xt[:, :], pb[:, :], ALU.add)
            s.dma(y[i * 128:(i + 1) * 128, :], xt[:, :])
        s.build()
    return nc


def build_sfull(R):
    nc = bass.Bass("TRN2", target_bir_lowering=False)
    x = nc.dram_tensor("x", [R, 64], F32, kind="ExternalInput").ap()
    al = nc.dram_tensor("al", [128, 64], F32, kind="ExternalInput").ap()
    y = nc.dram_tensor("y", [R, 64], F32, kind="ExternalOutput").ap()
    s = Ser(nc)
    with contextlib.ExitStack() as st:
        xt = st.enter_context(nc.sbuf_tensor("xt", [128, 64], F32))
        ea = st.enter_context(nc.sbuf_tensor("ea", [128, 64], F32))
        s.dma(ea[:, :], al[:, :])
        s.act(ea[:, :], ea[:, :], AF.Exp)
        for i in range(R // 128):
            s.dma(xt[:, :], x[i * 128:(i + 1) * 128, :])
            s.tt(xt[:, :], xt[:, :], ea[:, :], ALU.mult)
            s.act(xt[:, :], xt[:, :], AF.Copy, scale=-1.0)
            s.dma(y[i * 128:(i + 1) * 128, :], xt[:, :])
        s.build()
    return nc


def build_umul(R):
    nc = bass.Bass("TRN2", target_bir_lowering=False)
    xs = nc.dram_tensor("xs", [R, 4096], F32, kind="ExternalInput").ap()
    dt = nc.dram_tensor("dt", [R, 64], F32, kind="ExternalInput").ap()
    y = nc.dram_tensor("y", [R, 4096], F32, kind="ExternalOutput").ap()
    s = Ser(nc)
    with contextlib.ExitStack() as st:
        xt = st.enter_context(nc.sbuf_tensor("xt", [128, 4096], F32))
        dts = st.enter_context(nc.sbuf_tensor("dts", [128, 64], F32))
        ot = st.enter_context(nc.sbuf_tensor("ot", [128, 4096], F32))
        for i in range(R // 128):
            s.dma(xt[:, :], xs[i * 128:(i + 1) * 128, :])
            s.dma(dts[:, :], dt[i * 128:(i + 1) * 128, :])
            for h in range(64):
                s.ts(ot[:, h * 64:(h + 1) * 64], xt[:, h * 64:(h + 1) * 64],
                     dts[:, h:h + 1], ALU.mult)
            s.dma(y[i * 128:(i + 1) * 128, :], ot[:, :])
        s.build()
    return nc


def build_sincos(R):
    nc = bass.Bass("TRN2", target_bir_lowering=False)
    cum = nc.dram_tensor("cum", [R, 64], F32, kind="ExternalInput").ap()
    fx = nc.dram_tensor("fx", [128, 2048], F32, kind="ExternalInput").ap()
    sn = nc.dram_tensor("sn", [R, 2048], F32, kind="ExternalOutput").ap()
    cs = nc.dram_tensor("cs", [R, 2048], F32, kind="ExternalOutput").ap()
    s = Ser(nc)
    with contextlib.ExitStack() as st:
        ct = st.enter_context(nc.sbuf_tensor("ct", [128, 64], F32))
        fs = st.enter_context(nc.sbuf_tensor("fs", [128, 2048], F32))
        ph = st.enter_context(nc.sbuf_tensor("ph", [128, 2048], F32))
        t1 = st.enter_context(nc.sbuf_tensor("t1", [128, 2048], F32))
        fr = st.enter_context(nc.sbuf_tensor("fr", [128, 2048], F32))
        ob = st.enter_context(nc.sbuf_tensor("ob", [128, 2048], F32))
        s.dma(fs[:, :], fx[:, :])
        for i in range(R // 128):
            s.dma(ct[:, :], cum[i * 128:(i + 1) * 128, :])
            for h in range(64):
                s.ts(ph[:, h * 32:(h + 1) * 32], fs[:, h * 32:(h + 1) * 32],
                     ct[:, h:h + 1], ALU.mult)
            s.act(ph[:, :], ph[:, :], AF.Copy, scale=1.0 / TWO_PI)
            s.act(t1[:, :], ph[:, :], AF.Copy, bias=MAGIC)
            s.act(t1[:, :], t1[:, :], AF.Copy, bias=-MAGIC)
            s.tt(fr[:, :], ph[:, :], t1[:, :], ALU.subtract)
            s.act(ob[:, :], fr[:, :], AF.Sin, scale=TWO_PI)
            s.dma(sn[i * 128:(i + 1) * 128, :], ob[:, :])
            s.act(fr[:, :], fr[:, :], AF.Copy, bias=0.25)
            s.act(t1[:, :], fr[:, :], AF.Copy, bias=MAGIC)
            s.act(t1[:, :], t1[:, :], AF.Copy, bias=-MAGIC)
            s.tt(fr[:, :], fr[:, :], t1[:, :], ALU.subtract)
            s.act(ob[:, :], fr[:, :], AF.Sin, scale=TWO_PI)
            s.dma(cs[i * 128:(i + 1) * 128, :], ob[:, :])
        s.build()
    return nc


def build_rope(R):
    nc = bass.Bass("TRN2", target_bir_lowering=False)
    raw = nc.dram_tensor("raw", [R, 64], F32, kind="ExternalInput").ap()
    bx = nc.dram_tensor("bx", [128, 4096], F32, kind="ExternalInput").ap()
    sn = nc.dram_tensor("sn", [R, 2048], F32, kind="ExternalInput").ap()
    cs = nc.dram_tensor("cs", [R, 2048], F32, kind="ExternalInput").ap()
    y = nc.dram_tensor("y", [R, 4096], F32, kind="ExternalOutput").ap()
    s = Ser(nc)
    with contextlib.ExitStack() as st:
        rt = st.enter_context(nc.sbuf_tensor("rt", [128, 64], F32))
        bs = st.enter_context(nc.sbuf_tensor("bs", [128, 4096], F32))
        sq = st.enter_context(nc.sbuf_tensor("sq", [128, 64], F32))
        ms = st.enter_context(nc.sbuf_tensor("ms", [128, 1], F32))
        bh = st.enter_context(nc.sbuf_tensor("bh", [128, 4096], F32))
        sg = st.enter_context(nc.sbuf_tensor("sg", [128, 2048], F32))
        cg = st.enter_context(nc.sbuf_tensor("cg", [128, 2048], F32))
        s1 = st.enter_context(nc.sbuf_tensor("s1", [128, 2048], F32))
        s2 = st.enter_context(nc.sbuf_tensor("s2", [128, 2048], F32))
        ot = st.enter_context(nc.sbuf_tensor("ot", [128, 4096], F32))
        s.dma(bs[:, :], bx[:, :])
        for i in range(R // 128):
            s.dma(rt[:, :], raw[i * 128:(i + 1) * 128, :])
            s.dma(sg[:, :], sn[i * 128:(i + 1) * 128, :])
            s.dma(cg[:, :], cs[i * 128:(i + 1) * 128, :])
            s.act(sq[:, :], rt[:, :], AF.Square)
            s.red(ms[:, :], sq[:, :])
            s.op('scalar', lambda p, o=ms[:, :]: p.mul(o, o, 1.0 / 64))
            s.ts(ms[:, :], ms[:, :], 1e-6, ALU.add)
            s.act(ms[:, :], ms[:, :], AF.Ln)
            s.act(ms[:, :], ms[:, :], AF.Exp, scale=-0.5)
            s.ts(rt[:, :], rt[:, :], ms[:, :], ALU.mult)
            for h in range(64):
                s.tt(bh[:, h * 64:(h + 1) * 64], rt[:, :],
                     bs[:, h * 64:(h + 1) * 64], ALU.add)
            s.tt(s1[:, :], bh[:, 0:4096:2], cg[:, :], ALU.mult)
            s.tt(s2[:, :], bh[:, 1:4096:2], sg[:, :], ALU.mult)
            s.tt(ot[:, 0:4096:2], s1[:, :], s2[:, :], ALU.subtract)
            s.tt(s1[:, :], bh[:, 0:4096:2], sg[:, :], ALU.mult)
            s.tt(s2[:, :], bh[:, 1:4096:2], cg[:, :], ALU.mult)
            s.tt(ot[:, 1:4096:2], s1[:, :], s2[:, :], ALU.add)
            s.dma(y[i * 128:(i + 1) * 128, :], ot[:, :])
        s.build()
    return nc


def build_scan():
    nc = bass.Bass("TRN2", target_bir_lowering=False)
    Bt = nc.dram_tensor("Bt", [64, 1024], F32, kind="ExternalInput").ap()
    Ct = nc.dram_tensor("Ct", [64, 1024], F32, kind="ExternalInput").ap()
    Bk = nc.dram_tensor("Bk", [128, 512], F32, kind="ExternalInput").ap()
    U = nc.dram_tensor("U", [128, 512], F32, kind="ExternalInput").ap()
    scol = nc.dram_tensor("scol", [128, 8], F32, kind="ExternalInput").ap()
    srow = nc.dram_tensor("srow", [1, 1024], F32, kind="ExternalInput").ap()
    sendt = nc.dram_tensor("sendt", [128, 8], F32, kind="ExternalInput").ap()
    spendt = nc.dram_tensor("spendt", [128, 8], F32, kind="ExternalInput").ap()
    Sin = nc.dram_tensor("Sin", [64, 512], F32, kind="ExternalInput").ap()
    trim = nc.dram_tensor("trim", [128, 128], F32, kind="ExternalInput").ap()
    negm = nc.dram_tensor("negm", [128, 128], F32, kind="ExternalInput").ap()
    on128 = nc.dram_tensor("on128", [1, 128], F32, kind="ExternalInput").ap()
    on64 = nc.dram_tensor("on64", [1, 64], F32, kind="ExternalInput").ap()
    Y = nc.dram_tensor("Y", [128, 512], F32, kind="ExternalOutput").ap()
    Sout = nc.dram_tensor("Sout", [64, 512], F32, kind="ExternalOutput").ap()
    s = Ser(nc)
    with contextlib.ExitStack() as st:
        bt = st.enter_context(nc.sbuf_tensor("bt", [64, 1024], F32))
        ctl = st.enter_context(nc.sbuf_tensor("ctl", [64, 1024], F32))
        btb = st.enter_context(nc.sbuf_tensor("btb", [64, 1024], BF16))
        ctb = st.enter_context(nc.sbuf_tensor("ctb", [64, 1024], BF16))
        bk = st.enter_context(nc.sbuf_tensor("bk", [128, 512], F32))
        u = st.enter_context(nc.sbuf_tensor("u", [128, 512], F32))
        ub = st.enter_context(nc.sbuf_tensor("ub", [128, 512], BF16))
        sc = st.enter_context(nc.sbuf_tensor("sc", [128, 8], F32))
        scn = st.enter_context(nc.sbuf_tensor("scn", [128, 8], F32))
        sr = st.enter_context(nc.sbuf_tensor("sr", [1, 1024], F32))
        se = st.enter_context(nc.sbuf_tensor("se", [128, 8], F32))
        spn = st.enter_context(nc.sbuf_tensor("spn", [128, 8], F32))
        sin_ = st.enter_context(nc.sbuf_tensor("sin_", [64, 512], F32))
        sinb = st.enter_context(nc.sbuf_tensor("sinb", [64, 512], BF16))
        tr = st.enter_context(nc.sbuf_tensor("tr", [128, 128], F32))
        ng = st.enter_context(nc.sbuf_tensor("ng", [128, 128], F32))
        o128 = st.enter_context(nc.sbuf_tensor("o128", [1, 128], F32))
        o64 = st.enter_context(nc.sbuf_tensor("o64", [1, 64], F32))
        dif = st.enter_context(nc.sbuf_tensor("dif", [128, 128], F32))
        gm = st.enter_context(nc.sbuf_tensor("gm", [128, 128], F32))
        gmb = st.enter_context(nc.sbuf_tensor("gmb", [128, 128], BF16))
        er = st.enter_context(nc.sbuf_tensor("er", [1, 128], F32))
        cts = st.enter_context(nc.sbuf_tensor("cts", [64, 128], F32))
        ctsb = st.enter_context(nc.sbuf_tensor("ctsb", [64, 128], BF16))
        ec = st.enter_context(nc.sbuf_tensor("ec", [128, 1], F32))
        bsc = st.enter_context(nc.sbuf_tensor("bsc", [128, 64], F32))
        bscb = st.enter_context(nc.sbuf_tensor("bscb", [128, 64], BF16))
        ssc = st.enter_context(nc.sbuf_tensor("ssc", [64, 64], F32))
        atc = st.enter_context(nc.sbuf_tensor("atc", [64, 1], F32))
        yo = st.enter_context(nc.sbuf_tensor("yo", [128, 512], F32))
        so = st.enter_context(nc.sbuf_tensor("so", [64, 512], F32))
        pbc = st.enter_context(nc.psum_tensor([128, 128], F32))
        pg = st.enter_context(nc.psum_tensor([128, 128], F32))
        py = st.enter_context(nc.psum_tensor([128, 64], F32))
        pe = st.enter_context(nc.psum_tensor([64, 128], F32))
        psl = st.enter_context(nc.psum_tensor([64, 64], F32))
        for ap_, src in ((bt, Bt), (ctl, Ct), (bk, Bk), (u, U), (sc, scol),
                         (sr, srow), (se, sendt), (spn, spendt), (sin_, Sin),
                         (tr, trim), (ng, negm), (o128, on128), (o64, on64)):
            s.dma(ap_[:, :], src[:, :])
        s.copy(btb[:, :], bt[:, :])
        s.copy(ctb[:, :], ctl[:, :])
        s.copy(ub[:, :], u[:, :])
        s.copy(sinb[:, :], sin_[:, :])
        s.act(scn[:, :], sc[:, :], AF.Copy, scale=-1.0)
        s.act(spn[:, :], spn[:, :], AF.Copy, scale=-1.0)
        for h in range(8):
            h128 = slice(h * 128, (h + 1) * 128)
            h64 = slice(h * 64, (h + 1) * 64)
            s.mm(pbc[:, :], o128[:, :], sr[:, h128])
            s.ts(dif[:, :], pbc[:, :], scn[:, h:h + 1], ALU.add)
            s.tt(dif[:, :], dif[:, :], tr[:, :], ALU.mult)
            s.tt(dif[:, :], dif[:, :], ng[:, :], ALU.add)
            s.act(dif[:, :], dif[:, :], AF.Exp)
            s.mm(pg[:, :], btb[:, h128], ctb[:, h128])
            s.tt(gm[:, :], pg[:, :], dif[:, :], ALU.mult)
            s.copy(gmb[:, :], gm[:, :])
            s.act(er[:, :], sr[:, h128], AF.Exp, bias=spn[0:1, h:h + 1])
            s.mm(pe[:, :], o64[:, :], er[:, :])
            s.tt(cts[:, :], pe[:, :], ctl[:, h128], ALU.mult)
            s.copy(ctsb[:, :], cts[:, :])
            s.mm(py[:, :], gmb[:, :], ub[:, h64], start=True, stop=False)
            s.mm(py[:, :], ctsb[:, :], sinb[:, h64], start=False, stop=True)
            s.copy(yo[:, h64], py[:, :])
            s.act(ec[:, :], scn[:, h:h + 1], AF.Exp, bias=se[:, h:h + 1])
            s.ts(bsc[:, :], bk[:, h64], ec[:, :], ALU.mult)
            s.copy(bscb[:, :], bsc[:, :])
            s.mm(psl[:, :], bscb[:, :], ub[:, h64])
            s.act(atc[:, :], se[0:64, h:h + 1], AF.Exp, bias=spn[0:64, h:h + 1])
            s.ts(ssc[:, :], sin_[:, h64], atc[:, :], ALU.mult)
            s.tt(so[:, h64], ssc[:, :], psl[:, :], ALU.add)
        s.dma(Y[:, :], yo[:, :])
        s.dma(Sout[:, :], so[:, :])
        s.build()
    return nc


def build_ygate(R):
    nc = bass.Bass("TRN2", target_bir_lowering=False)
    ys = nc.dram_tensor("ys", [R, 4096], F32, kind="ExternalInput").ap()
    xs = nc.dram_tensor("xs", [R, 4096], F32, kind="ExternalInput").ap()
    z = nc.dram_tensor("z", [R, 4096], F32, kind="ExternalInput").ap()
    dx = nc.dram_tensor("dx", [128, 4096], F32, kind="ExternalInput").ap()
    nw = nc.dram_tensor("nw", [128, 4096], F32, kind="ExternalInput").ap()
    y = nc.dram_tensor("y", [R, 4096], F32, kind="ExternalOutput").ap()
    s = Ser(nc)
    with contextlib.ExitStack() as st:
        yt = st.enter_context(nc.sbuf_tensor("yt", [128, 4096], F32))
        xt = st.enter_context(nc.sbuf_tensor("xt", [128, 4096], F32))
        zt = st.enter_context(nc.sbuf_tensor("zt", [128, 4096], F32))
        ds = st.enter_context(nc.sbuf_tensor("ds", [128, 4096], F32))
        ns = st.enter_context(nc.sbuf_tensor("ns", [128, 4096], F32))
        sq = st.enter_context(nc.sbuf_tensor("sq", [128, 4096], F32))
        ms = st.enter_context(nc.sbuf_tensor("ms", [128, 1], F32))
        s.dma(ds[:, :], dx[:, :])
        s.dma(ns[:, :], nw[:, :])
        for i in range(R // 128):
            s.dma(yt[:, :], ys[i * 128:(i + 1) * 128, :])
            s.dma(xt[:, :], xs[i * 128:(i + 1) * 128, :])
            s.dma(zt[:, :], z[i * 128:(i + 1) * 128, :])
            s.tt(xt[:, :], xt[:, :], ds[:, :], ALU.mult)
            s.stt(yt[:, :], yt[:, :], 0.125, xt[:, :], ALU.mult, ALU.add)
            s.act(zt[:, :], zt[:, :], AF.Silu)
            s.tt(yt[:, :], yt[:, :], zt[:, :], ALU.mult)
            s.act(sq[:, :], yt[:, :], AF.Square)
            s.red(ms[:, :], sq[:, :])
            s.op('scalar', lambda p, o=ms[:, :]: p.mul(o, o, 1.0 / 4096))
            s.ts(ms[:, :], ms[:, :], 1e-6, ALU.add)
            s.act(ms[:, :], ms[:, :], AF.Ln)
            s.act(ms[:, :], ms[:, :], AF.Exp, scale=-0.5)
            s.ts(yt[:, :], yt[:, :], ms[:, :], ALU.mult)
            s.tt(yt[:, :], yt[:, :], ns[:, :], ALU.mult)
            s.dma(y[i * 128:(i + 1) * 128, :], yt[:, :])
        s.build()
    return nc


def tile128(v):
    v = np.asarray(v, np.float32).reshape(1, -1)
    return cc(np.tile(v, (128, 1)))


def shard_rows(x):
    Rc = x.shape[0] // NCORES
    return [cc(x[i * Rc:(i + 1) * Rc]) for i in range(NCORES)]


def elemwise_launch(key, builder, per_core_named, shared_named):
    ins = []
    for i in range(NCORES):
        d = {k: v[i] for k, v in per_core_named.items()}
        d.update(shared_named)
        ins.append(d)
    return _launch(key, builder, ins)


def mamba_block(xn, W):
    zxbcdt = mm_shardM(xn, W["W_in"], np.zeros(8384, np.float32))
    z = zxbcdt[:, :4096]
    xs = zxbcdt[:, 4096:8192]
    Braw = zxbcdt[:, 8192:8256]
    Craw = zxbcdt[:, 8256:8320]
    dtraw = zxbcdt[:, 8320:8384]
    Rc = T // NCORES
    res = elemwise_launch(("softplus", Rc), lambda: build_softplus(Rc),
                          {"x": shard_rows(dtraw)}, {"bt": tile128(W["dt_bias"])})
    dt = np.concatenate([r["y"] for r in res], axis=0)
    tri = cc(np.triu(np.ones((128, 128), np.float32)))
    onc = cc(np.ones((128, 1), np.float32))
    onr = cc(np.ones((1, 128), np.float32))
    res = elemwise_launch(("cumloc", Rc), lambda: build_cumloc(Rc),
                          {"x": shard_rows(dt)}, {"tri": tri, "onc": onc, "onr": onr})
    cumloc = [r["y"] for r in res]
    tots = cc(np.concatenate([r["tout"] for r in res], axis=0))
    masks = [cc(np.array([[1.0] if j < i else [0.0] for j in range(8)], np.float32))
             for i in range(NCORES)]
    ins = [{"x": cumloc[i], "tots": tots, "msk": masks[i], "onr": onr}
           for i in range(NCORES)]
    res = _launch(("cumglob", Rc), lambda: build_cumglob(Rc), ins)
    cum = np.concatenate([r["y"] for r in res], axis=0)
    res = elemwise_launch(("sfull", Rc), lambda: build_sfull(Rc),
                          {"x": shard_rows(cum)}, {"al": tile128(W["A_log"])})
    sfull = np.concatenate([r["y"] for r in res], axis=0)
    res = elemwise_launch(("umul", Rc), lambda: build_umul(Rc),
                          {"xs": shard_rows(xs), "dt": shard_rows(dt)}, {})
    u = np.concatenate([r["y"] for r in res], axis=0)
    fexp = tile128(np.asarray(W["rope_freqs"], np.float32).reshape(-1))
    res = elemwise_launch(("sincos", Rc), lambda: build_sincos(Rc),
                          {"cum": shard_rows(cum)}, {"fx": fexp})
    sinp = np.concatenate([r["sn"] for r in res], axis=0)
    cosp = np.concatenate([r["cs"] for r in res], axis=0)
    bbias = tile128(np.asarray(W["B_bias"], np.float32).reshape(-1))
    cbias = tile128(np.asarray(W["C_bias"], np.float32).reshape(-1))
    res = elemwise_launch(("rope", Rc), lambda: build_rope(Rc),
                          {"raw": shard_rows(Braw), "sn": shard_rows(sinp),
                           "cs": shard_rows(cosp)}, {"bx": bbias})
    Bh = np.concatenate([r["y"] for r in res], axis=0)
    res = elemwise_launch(("rope", Rc), lambda: build_rope(Rc),
                          {"raw": shard_rows(Craw), "sn": shard_rows(sinp),
                           "cs": shard_rows(cosp)}, {"bx": cbias})
    Ch = np.concatenate([r["y"] for r in res], axis=0)
    trim = cc(np.tril(np.ones((128, 128), np.float32)))
    trim = cc(np.triu(np.ones((128, 128), np.float32)))
    negm = cc(np.where(np.triu(np.ones((128, 128), np.float32)) > 0, 0.0, -1e30)
              .astype(np.float32))
    on128 = cc(np.ones((1, 128), np.float32))
    on64 = cc(np.ones((1, 64), np.float32))
    S = [np.zeros((64, 512), np.float32) for _ in range(NCORES)]
    ys = np.zeros((T, 4096), np.float32)
    nchunks = T // 128
    for c in range(nchunks):
        t0 = c * 128
        pend_idx = max(t0 - 1, 0)
        ins = []
        for k in range(NCORES):
            hs = slice(k * 8 * 64, (k + 1) * 8 * 64)
            hh = slice(k * 8, (k + 1) * 8)
            Bblk = Bh[t0:t0 + 128, hs]
            Cblk = Ch[t0:t0 + 128, hs]
            Bt = np.concatenate([Bblk[:, j * 64:(j + 1) * 64].T for j in range(8)],
                                axis=1)
            Ct = np.concatenate([Cblk[:, j * 64:(j + 1) * 64].T for j in range(8)],
                                axis=1)
            sc = sfull[t0:t0 + 128, hh]
            srow = np.concatenate([sc[:, j].reshape(1, 128) for j in range(8)],
                                  axis=1)
            send = sfull[t0 + 127, hh].reshape(1, 8)
            spend = sfull[pend_idx, hh].reshape(1, 8)
            ins.append({
                "Bt": cc(Bt), "Ct": cc(Ct), "Bk": cc(Bblk), "U": cc(u[t0:t0 + 128, hs]),
                "scol": cc(sc), "srow": cc(srow),
                "sendt": cc(np.tile(send, (128, 1))),
                "spendt": cc(np.tile(spend, (128, 1))),
                "Sin": S[k], "trim": trim, "negm": negm,
                "on128": on128, "on64": on64,
            })
        res = _launch(("scan",), build_scan, ins)
        for k in range(NCORES):
            hs = slice(k * 8 * 64, (k + 1) * 8 * 64)
            ys[t0:t0 + 128, hs] = res[k]["Y"]
            S[k] = cc(res[k]["Sout"])
    res = elemwise_launch(("ygate", Rc), lambda: build_ygate(Rc),
                          {"ys": shard_rows(ys), "xs": shard_rows(xs),
                           "z": shard_rows(z)},
                          {"dx": tile128(np.repeat(np.asarray(W["D_param"], np.float32), 64)),
                           "nw": tile128(W["norm_w"])})
    gn = np.concatenate([r["y"] for r in res], axis=0)
    out = mm_shardM(gn, W["W_out"], np.zeros(D, np.float32))
    return out


def kernel(**inputs):
    ffnW = {k[len("L0_ffn_"):]: np.asarray(v, np.float32)
            for k, v in inputs.items() if k.startswith("L0_ffn_")}
    atW = {k[len("L0_attn_"):]: np.asarray(v, np.float32)
           for k, v in inputs.items() if k.startswith("L0_attn_")}
    x = cc(np.asarray(inputs["L0_attn_x"], np.float32).reshape(T, D))

    ln0 = ln_shard(x, eps=1e-5)
    f1 = ffn_block(ln0, ffnW)
    x1 = axpy(x, f1, 0.5)

    ln1 = ln_shard(x1, eps=1e-5)
    m = mamba_block(ln1, atW)
    x2 = axpy(x1, m, 1.0)

    ln2 = ln_shard(x2, eps=1e-5)
    f2 = ffn_block(ln2, ffnW)
    x3 = axpy(x2, f2, 0.5)

    return np.ascontiguousarray(x3.reshape(1, T, D).astype(np.float32))
